# version 03
# baseline (speedup 1.0000x reference)
import numpy as np
import ml_dtypes
import concourse.bass as bass
import concourse.mybir as mybir
from concourse.bass_utils import run_bass_kernel_spmd

F32 = mybir.dt.float32
BF16 = mybir.dt.bfloat16
AF = mybir.ActivationFunctionType
ALU = mybir.AluOpType
AXX = mybir.AxisListType.X

D, S, NQ, NH, FF = 1024, 4096, 2048, 16, 4096
NCH = 8
INV_SQ8 = float(1.0 / np.sqrt(8.0))
KC_GROUPS = [(3 * i, 3) for i in range(10)] + [(30, 2)]
NGRP = len(KC_GROUPS)
NQB, AQB, BIS_ITERS = 8, 4, 34
P1_STATS_ONLY = False
ENGS = ("sync", "gpsimd", "tensor", "vector", "scalar")


def build_program(upto=None):
    nc = bass.Bass("TRN2", target_bir_lowering=False,
                   dynamic_dma_scratch_size=16384)
    dt = nc.dram_tensor
    xTp = dt("xTp", [D, S], F32, kind="ExternalInput")
    xQ = dt("xQ", [128, 32768], F32, kind="ExternalInput")
    qkwC = dt("qkwC", [128, 8192], F32, kind="ExternalInput")
    vwT = dt("vwT", [D, D], F32, kind="ExternalInput")
    vb = dt("vb", [D], F32, kind="ExternalInput")
    SMd = dt("SMd", [128, 128], F32, kind="ExternalInput")
    W1C = dt("W1C", [128, 32768], F32, kind="ExternalInput")
    W2C = dt("W2C", [128, 32768], F32, kind="ExternalInput")
    out = dt("out", [D, NQ], F32, kind="ExternalOutput")
    cc_in = dt("cc_in", [1, NQ], F32, kind="Internal")
    cc_out = dt("cc_out", [2, NQ], F32, kind="Internal")

    cur = [((nc.sbuf_base + 63) // 64) * 64]

    def salloc(name, shape, dtype):
        nb = shape[1] * (2 if dtype == BF16 else 4)
        at = ((cur[0] + 31) // 32) * 32
        cur[0] = at + nb
        return nc.alloc_sbuf_tensor_at(name, shape, dtype, offset=at), at

    def oalloc(name, shape, dtype, at):
        return nc.alloc_sbuf_tensor_at(name, shape, dtype, offset=at)

    ones_b, _ = salloc("ones_b", [128, 128], BF16)
    ones_f, _ = salloc("ones_f", [128, 128], F32)
    SMt, _ = salloc("SMt", [128, 128], F32)
    bst, _ = salloc("bst", [128, 14], F32)
    aslc, _ = salloc("aslc", [128, 1], BF16)
    bsti, _ = salloc("bsti", [128, 1], mybir.dt.int32)
    rowA, _ = salloc("rowA", [128, 4096], F32)
    xy, xyb = salloc("xy", [128, 30720], BF16)
    qkA, qab = salloc("qkA", [128, 32768], BF16)
    vW, vwb = salloc("vW", [128, 33280], BF16)
    assert cur[0] <= nc.sbuf_top, f"SBUF overflow {cur[0]} > {nc.sbuf_top}"
    got = nc.bump_sbuf(cur[0] - nc.sbuf_base)
    assert got is not None, "could not reserve manual SBUF region"

    xbu = [oalloc(f"xb{i}", [128, 4096], BF16, xyb + 8192 * i) for i in range(2)]
    y1u = [oalloc(f"y1{i}", [128, 4096], BF16, xyb + 16384 + 8192 * i) for i in range(2)]
    sqb = oalloc("sqb", [128, 4096], BF16, xyb + 32768)
    qkwS = [oalloc(f"qkwS{i}", [128, 1024], BF16, xyb + 40960 + 2048 * i) for i in range(2)]
    vwS = [oalloc(f"vwS{i}", [128, 1024], BF16, xyb + 45056 + 2048 * i) for i in range(2)]
    dnD = oalloc("dnD", [128, 4096], F32, xyb + 0)
    dnN = oalloc("dnN", [128, 4096], F32, xyb + 16384)
    dnG = oalloc("dnG", [128, 4096], F32, xyb + 32768)
    ukT = oalloc("ukT", [128, 4096], BF16, xyb + 49152)
    uqT2 = oalloc("uqT2", [128, 2048], BF16, xyb + 57344)
    vbrow = oalloc("vbrow", [128, 1024], BF16, xyb + 57344)
    sq2h = [oalloc(f"sq2h{i}", [128, 1024], BF16, xyb + 57344 + 2048 * i) for i in range(2)]
    amat = oalloc("amat", [128, 240], BF16, xyb + 49152)
    bmat = oalloc("bmat", [128, 240], BF16, xyb + 49664)
    rfull = oalloc("rfull", [32, 128], F32, xyb + 50176)
    tbuf = oalloc("tbuf", [32, 64], F32, xyb + 50688)
    scrF = oalloc("scrF", [128, 512], F32, xyb + 50944)
    wrcol = oalloc("wrcol", [128, 8], BF16, xyb + 52992)
    qkhK = [oalloc(f"qkhK{i}", [128, 4096], BF16, xyb + 8192 * i) for i in range(2)]
    qkhQ = [oalloc(f"qkhQ{i}", [128, 2048], BF16, xyb + 16384 + 4096 * i) for i in range(2)]
    Pbu = [oalloc(f"Pb{i}", [128, 1536], BF16, xyb + 24576 + 3072 * i) for i in range(2)]
    xsu = [oalloc(f"xs{i}", [128, 512], F32, xyb + 30720 + 2048 * i) for i in range(2)]
    ctmp = oalloc("ctmp", [128, 512], F32, xyb + 34816)
    den_sb = oalloc("den_sb", [128, 1024], F32, xyb + 36864)
    otmp = oalloc("otmp", [128, 512], BF16, xyb + 40960)
    ln2m = oalloc("ln2m", [128, 1024], F32, xyb + 43008)
    y2t = oalloc("y2t", [128, 16384], BF16, xyb + 0)
    hbf = oalloc("hbf", [128, 8192], BF16, xyb + 32768)
    W1v = oalloc("W1v", [128, 32768], BF16, vwb)
    W2s = [(oalloc(f"W2a{i}", [128, 2048], BF16, qab + 16384 * i + 4096),
            oalloc(f"W2b{i}", [128, 2048], BF16, qab + 16384 * i + 12288))
           for i in range(2)]
    outbu = [oalloc(f"outb{i}", [128, 512], F32, qab + 8192 * 4 + 4096 + 2048 * i) for i in range(2)]
    xs2u = [oalloc(f"xs2{i}", [128, 512], F32, qab + 8192 * 5 + 4096 + 2048 * i) for i in range(2)]
    msig_sb = oalloc("msig_sb", [128, 512], F32, qab + 8192 * 6 + 4096)
    ubuf = oalloc("ubuf", [128, 512], F32, qab + 8192 * 7 + 4096)

    psA_g = nc.psum_tensor("psA", [128, 2048], F32)
    psB_g = nc.psum_tensor("psB", [128, 2048], F32)
    psA = psA_g.__enter__()
    psB = psB_g.__enter__()

    names = ["sLX", "sWQ", "sWV", "sSQ", "sST", "sSS", "sRM", "sBC", "sY1",
             "sQK", "sQE", "sVI", "sVM", "sVE", "sSQ2", "sDN", "sP2", "sU",
             "sBH", "sSC", "sEXP", "sCTG", "sCTX", "sEV1", "sEV2", "sEV3",
             "sSH", "sXS", "sX1B", "sSQ3", "sST2", "sSS2", "sL2", "sL2S",
             "sBC2", "sY2", "sRR", "sRRE", "sCC", "sMSG", "sF1", "sGE",
             "sF2", "sFB", "sFE", "sW1", "sW2", "sOUT", "sMS", "sNM"]
    guards = [nc.semaphore(n) for n in names]
    sem = {n: g.__enter__() for n, g in zip(names, guards)}

    Q = {e: [] for e in ENGS}
    C = {n: 0 for n in names}
    marks = {}

    def mark(ph):
        marks[ph] = ({e: len(Q[e]) for e in ENGS}, dict(C))

    def emit(eng, fn):
        Q[eng].append(fn)

    def W(eng, sn, val):
        if val > 0:
            emit(eng, lambda e, ss=sem[sn], v=val: e.wait_ge(ss, v))

    def INC(sn, by=1):
        C[sn] += by
        return sem[sn]

    gd_hist = []

    def dma(eng, o, i, sn, slow=False):
        if eng == "gpsimd":
            if len(gd_hist) >= 3:
                psn, pval = gd_hist[-3]
                W("gpsimd", psn, pval)
        ss = INC(sn, 16)
        if eng == "gpsimd":
            gd_hist.append((sn, C[sn]))
        if slow:
            emit(eng, lambda e, oo=o, ii=i, s=ss: e.dma_start(
                out=oo, in_=ii, allow_slow_non_contiguous=True).then_inc(s, 16))
        else:
            emit(eng, lambda e, oo=o, ii=i, s=ss: e.dma_start(out=oo, in_=ii).then_inc(s, 16))

    def AP(h, off, dims):
        return bass.AP(h, off, [list(d) for d in dims])

    def ctxbf(c, qb, p0=0, pn=128):
        return AP(qkA, 32768 * p0 + 4096 * c + 512 * qb, [[32768, pn], [1, 512]])

    def x1bsl(c, qb):
        return AP(qkA, 4096 * c + 2048 + 512 * qb, [[32768, 128], [1, 512]])

    dma("sync", SMt[:, 0:128], AP(SMd, 0, [[128, 128], [1, 128]]), "sMS")
    dma("sync", den_sb[0:1, 0:1024], AP(vb, 0, [[1024, 1], [1, 1024]]), "sMS")
    ms_dmas = C["sMS"]
    W("vector", "sMS", ms_dmas)
    for f in [lambda e: e.memset(ones_b[:, :], 1.0),
              lambda e: e.memset(ones_f[:, :], 1.0),
              lambda e: e.memset(bst[:, 10:11], -4096.0),
              lambda e: e.memset(bst[:, 9:10], 1e-5),
              lambda e: e.memset(amat[:, 0:240], 0.0),
              lambda e: e.memset(bmat[:, 0:240], 0.0),
              lambda e: e.memset(AP(vW, 64, [[33280, 128], [1040, 32], [65, 16]]), 1.0)]:
        emit("vector", f)
    emit("vector", lambda e: e.tensor_copy(aslc[:, 0:1], SMt[:, 56:57]))
    emit("vector", lambda e: e.tensor_copy(vbrow[0:1, 0:1024], den_sb[0:1, 0:1024]))
    emit("vector", lambda e: e.tensor_add(bst[0:16, 7:8], SMt[0:16, 57:58], SMt[0:16, 58:59]))
    emit("vector", lambda e: e.tensor_scalar(bst[0:16, 8:9], SMt[0:16, 57:58], -1.0, None, ALU.mult))
    emit("vector", lambda e: e.tensor_scalar(bst[0:16, 12:13], SMt[0:16, 58:59], -1.0, None, ALU.mult))
    for c in range(NCH):
        j = 32 * c + 2 * c
        emit("vector", lambda e, j=j: e.tensor_copy(amat[0:64, j:j + 1], aslc[0:64, 0:1]))
        emit("vector", lambda e, j=j: e.tensor_copy(amat[64:128, j + 1:j + 2], aslc[64:128, 0:1]))
        emit("vector", lambda e, j=j: e.tensor_copy(bmat[0:64, j:j + 1], ones_b[0:64, 0:1]))
        emit("vector", lambda e, j=j: e.tensor_copy(bmat[64:128, j + 1:j + 2], ones_b[64:128, 0:1]))
    ss = INC("sMS")
    emit("vector", lambda e, s=ss: e.tensor_copy(bst[0:1, 11:12], bst[0:1, 9:10]).then_inc(s, 1))
    ms_all = C["sMS"]
    ms_amat = C["sMS"]

    mark("p0")
    stq, sqq = psA[0:1, 0:512], psA[0:1, 512:1024]
    mbc, rbc = psA[:, 1024:1536], psA[:, 1536:2048]

    def vacc(tt, vo):
        ps = psA if tt < 2 else psB
        o = 1024 * (tt % 2) + 512 * vo
        return ps[:, o:o + 512]

    for qb in range(NQB):
        xb, yy = xbu[qb % 2], y1u[qb % 2]
        if qb >= 2:
            W("gpsimd", "sY1", qb - 1)
        dma("gpsimd", AP(xbu[qb % 2], 0, [[4096, 128], [1, 4096]]),
            AP(xQ, 4096 * qb, [[32768, 128], [1, 4096]]), "sLX")
        for oc in range(NCH):
            k = qb * NCH + oc
            if k >= 2:
                W("gpsimd", "sQK", k - 1)
            dma("gpsimd", AP(qkwS[oc % 2], 0, [[1024, 128], [1, 1024]]),
                AP(qkwC, 1024 * oc, [[8192, 128], [1, 1024]]), "sWQ")
        for ic in range(NCH):
            k = qb * NCH + ic
            if k >= 2:
                W("gpsimd", "sVI", k - 1)
            dma("gpsimd", AP(vwS[ic % 2], 0, [[1024, 128], [1, 1024]]),
                AP(vwT, 1024 * 128 * ic, [[1024, 128], [1, 1024]]), "sWV")

        W("vector", "sLX", 16 * (qb + 1))
        if qb >= 1:
            W("vector", "sST", qb)
        ss = INC("sSQ")
        emit("vector", lambda e, a=xb, s=ss: e.tensor_mul(sqb[:, :], a[:, :], a[:, :]).then_inc(s, 1))

        W("tensor", "sLX", 16 * (qb + 1))
        W("tensor", "sSQ", qb + 1)
        if qb >= 1:
            W("tensor", "sRM", 2 * qb)
            W("tensor", "sVE", 4 * qb)
        else:
            W("tensor", "sMS", ms_all)
        for ic in range(NCH):
            emit("tensor", lambda e, ic=ic, a=xb:
                 e.matmul(stq, ones_b[:, 0:1], a[:, 512 * ic:512 * (ic + 1)],
                          start=(ic == 0), stop=(ic == NCH - 1)))
        for ic in range(NCH):
            if ic < NCH - 1:
                emit("tensor", lambda e, ic=ic:
                     e.matmul(sqq, ones_b[:, 0:1], sqb[:, 512 * ic:512 * (ic + 1)],
                              start=(ic == 0), stop=False))
            else:
                ss = INC("sST")
                emit("tensor", lambda e, ic=ic, s=ss:
                     e.matmul(sqq, ones_b[:, 0:1], sqb[:, 512 * ic:512 * (ic + 1)],
                              start=False, stop=True).then_inc(s, 1))

        W("vector", "sST", qb + 1)
        emit("vector", lambda e: e.tensor_copy(rowA[0:1, 3072:3584], stq))
        emit("vector", lambda e: e.tensor_copy(rowA[0:1, 3584:4096], sqq))
        emit("vector", lambda e: e.tensor_scalar(rowA[0:1, 0:512], stq, 1.0 / D, None, ALU.mult))
        emit("vector", lambda e: e.tensor_scalar(rowA[0:1, 512:1024], sqq, 1.0 / D, None, ALU.mult))
        emit("vector", lambda e: e.tensor_mul(rowA[0:1, 1024:1536], rowA[0:1, 0:512], rowA[0:1, 0:512]))
        ss = INC("sRM")
        emit("vector", lambda e, s=ss: e.tensor_sub(rowA[0:1, 1536:2048], rowA[0:1, 512:1024],
                                                    rowA[0:1, 1024:1536]).then_inc(s, 1))
        W("scalar", "sRM", 2 * qb + 1)
        ss = INC("sSS")
        emit("scalar", lambda e, s=ss: e.activation(rowA[0:1, 2048:2560], rowA[0:1, 1536:2048],
                                                    AF.Sqrt, bias=bst[0:1, 9:10], scale=1.0).then_inc(s, 1))
        W("vector", "sSS", qb + 1)
        ss = INC("sRM")
        emit("vector", lambda e, s=ss: e.reciprocal(rowA[0:1, 2560:3072], rowA[0:1, 2048:2560]).then_inc(s, 1))

        if P1_STATS_ONLY:
            continue
        W("tensor", "sRM", 2 * (qb + 1))
        emit("tensor", lambda e: e.matmul(mbc, ones_f[0:1, 0:128], rowA[0:1, 0:512], start=True, stop=True))
        ss = INC("sBC")
        emit("tensor", lambda e, s=ss: e.matmul(rbc, ones_f[0:1, 0:128], rowA[0:1, 2560:3072],
                                                start=True, stop=True).then_inc(s, 1))

        W("vector", "sBC", qb + 1)
        for ic in range(NCH):
            sl = slice(512 * ic, 512 * (ic + 1))
            emit("vector", lambda e, a=xb, o=yy, sl=sl: e.tensor_sub(o[:, sl], a[:, sl], mbc))
            if ic < NCH - 1:
                emit("vector", lambda e, o=yy, sl=sl: e.tensor_mul(o[:, sl], o[:, sl], rbc))
            else:
                ss = INC("sY1")
                emit("vector", lambda e, o=yy, sl=sl, s=ss: e.tensor_mul(o[:, sl], o[:, sl], rbc).then_inc(s, 1))

        W("tensor", "sY1", qb + 1)
        for oc in range(NCH):
            k = qb * NCH + oc
            W("tensor", "sWQ", 16 * (k + 1))
            if k >= 2:
                W("tensor", "sQE", k - 1)
            qkp = psB[:, 512 * (oc % 2):512 * (oc % 2) + 512]
            for ic in range(NCH):
                if ic < NCH - 1:
                    emit("tensor", lambda e, oc=oc, ic=ic, p=qkp, y=yy:
                         e.matmul(p, qkwS[oc % 2][:, 128 * ic:128 * (ic + 1)],
                                  y[:, 512 * ic:512 * (ic + 1)], start=(ic == 0), stop=False))
                else:
                    ss = INC("sQK")
                    emit("tensor", lambda e, oc=oc, ic=ic, p=qkp, y=yy, s=ss:
                         e.matmul(p, qkwS[oc % 2][:, 128 * ic:128 * (ic + 1)],
                                  y[:, 512 * ic:512 * (ic + 1)], start=False, stop=True).then_inc(s, 1))
            W("vector", "sQK", k + 1)
            ss = INC("sQE")
            emit("vector", lambda e, oc=oc, qb=qb, p=qkp, s=ss:
                 e.tensor_scalar(qkA[:, 4096 * oc + 512 * qb:4096 * oc + 512 * (qb + 1)],
                                 p, SMt[:, oc:oc + 1], INV_SQ8, ALU.add, ALU.mult).then_inc(s, 1))

        W("tensor", "sQE", (qb + 1) * NCH)
        for ic in range(NCH):
            W("tensor", "sWV", 16 * (qb * NCH + ic + 1))
            for tt in range(4):
                for vo in range(2):
                    lastm = (tt == 3 and vo == 1)
                    if not lastm:
                        emit("tensor", lambda e, ic=ic, tt=tt, vo=vo, y=yy:
                             e.matmul(vacc(tt, vo),
                                      y[:, 512 * ic + 128 * tt:512 * ic + 128 * (tt + 1)],
                                      vwS[ic % 2][:, 512 * vo:512 * (vo + 1)],
                                      start=(ic == 0), stop=False))
                    else:
                        ss = INC("sVI")
                        emit("tensor", lambda e, ic=ic, tt=tt, vo=vo, y=yy, s=ss:
                             e.matmul(vacc(tt, vo),
                                      y[:, 512 * ic + 128 * tt:512 * ic + 128 * (tt + 1)],
                                      vwS[ic % 2][:, 512 * vo:512 * (vo + 1)],
                                      start=(ic == 0), stop=False).then_inc(s, 1))
        for tt in range(4):
            for vo in range(2):
                if not (tt == 3 and vo == 1):
                    emit("tensor", lambda e, tt=tt, vo=vo:
                         e.matmul(vacc(tt, vo), ones_b[0:1, 0:128],
                                  vbrow[0:1, 512 * vo:512 * (vo + 1)], start=False, stop=True))
                else:
                    ss = INC("sVM")
                    emit("tensor", lambda e, tt=tt, vo=vo, s=ss:
                         e.matmul(vacc(tt, vo), ones_b[0:1, 0:128],
                                  vbrow[0:1, 512 * vo:512 * (vo + 1)], start=False, stop=True).then_inc(s, 1))
        W("vector", "sVM", qb + 1)
        for tt in range(4):
            tg = qb * 4 + tt
            ps = psA if tt < 2 else psB
            ss = INC("sVE")
            emit("vector", lambda e, tt=tt, tg=tg, ps=ps, s=ss:
                 e.tensor_copy(AP(vW, 1040 * tg, [[33280, 128], [65, 16], [1, 64]]),
                               ps[:, 1024 * (tt % 2):1024 * (tt % 2) + 1024]).then_inc(s, 1))

    mark("p1")
    dps, nps = psA[0:16, 0:512], psA[0:16, 512:1024]
    sq2f = 0
    nmf = 0
    for tb in range(8):
        if tb == 0:
            W("vector", "sQE", 64)
            W("tensor", "sVE", 4 * NQB)
            W("tensor", "sMS", ms_amat)
        if tb >= 1:
            W("tensor", "sP2", tb)
        for f in range(4):
            if sq2f >= 2:
                W("vector", "sNM", 2 * sq2f - 2)
            for j in range(2):
                c = 2 * f + j
                qsl = qkA[:, 4096 * c + 512 * tb:4096 * c + 512 * (tb + 1)]
                if j == 0:
                    emit("vector", lambda e, q=qsl, f=f: e.tensor_mul(sq2h[f % 2][:, 0:512], q, q))
                else:
                    ss = INC("sSQ2")
                    emit("vector", lambda e, q=qsl, f=f, s=ss:
                         e.tensor_mul(sq2h[f % 2][:, 512:1024], q, q).then_inc(s, 1))
            sq2f += 1
            W("tensor", "sSQ2", sq2f)
            for j in range(2):
                c = 2 * f + j
                last = (c == NCH - 1)
                ss = INC("sNM") if True else None
                emit("tensor", lambda e, c=c, f=f, j=j, last=last, s=ss:
                     e.matmul(nps, bmat[:, 32 * c:32 * c + 16],
                              sq2h[f % 2][:, 512 * j:512 * (j + 1)],
                              start=(c == 0), stop=last).then_inc(s, 1))
            nmf += 2
        for c in range(NCH):
            if c < NCH - 1:
                emit("tensor", lambda e, c=c, tb=tb:
                     e.matmul(dps, amat[:, 32 * c:32 * c + 16],
                              qkA[:, 4096 * c + 512 * tb:4096 * c + 512 * (tb + 1)],
                              start=(c == 0), stop=False))
            else:
                ss = INC("sDN")
                emit("tensor", lambda e, c=c, tb=tb, s=ss:
                     e.matmul(dps, amat[:, 32 * c:32 * c + 16],
                              qkA[:, 4096 * c + 512 * tb:4096 * c + 512 * (tb + 1)],
                              start=False, stop=True).then_inc(s, 1))
        W("vector", "sDN", tb + 1)
        W("vector", "sNM", 8 * (tb + 1))
        emit("vector", lambda e, tb=tb: e.tensor_copy(dnD[0:16, 512 * tb:512 * (tb + 1)], dps))
        ss = INC("sP2")
        emit("vector", lambda e, tb=tb, s=ss: e.tensor_copy(dnN[0:16, 512 * tb:512 * (tb + 1)], nps).then_inc(s, 1))

    mark("p2a")
    ss = INC("sSQ3")
    emit("vector", lambda e, s=ss: e.tensor_reduce(bst[0:16, 0:1], dnN[0:16, :], AXX, ALU.max).then_inc(s, 1))
    W("scalar", "sSQ3", 1)
    ss = INC("sSQ3")
    emit("scalar", lambda e, s=ss: e.activation(bst[0:16, 1:2], bst[0:16, 0:1], AF.Sqrt,
                                                scale=16.0 / 9.0).then_inc(s, 1))
    emit("scalar", lambda e: e.activation(bst[0:16, 2:3], bst[0:16, 0:1], AF.Copy,
                                          scale=16.0 / 9.0))
    ss = INC("sSS2")
    emit("scalar", lambda e, s=ss: e.activation(bst[0:16, 5:6], bst[0:16, 0:1], AF.Square,
                                                scale=16.0 / 9.0 / 1.4142135623730951).then_inc(s, 1))
    W("vector", "sSS2", 1)
    emit("vector", lambda e: e.tensor_mul(bst[0:16, 4:5], bst[0:16, 1:2], bst[0:16, 2:3]))
    emit("vector", lambda e: e.tensor_mul(bst[0:16, 6:7], bst[0:16, 5:6], bst[0:16, 7:8]))
    emit("vector", lambda e: e.tensor_mul(bst[0:16, 3:4], bst[0:16, 2:3], bst[0:16, 8:9]))
    emit("vector", lambda e: e.tensor_scalar(uqT2[0:16, 0:2048], dnD[0:16, 0:2048], 0.0, None, ALU.is_ge))
    emit("vector", lambda e: e.tensor_scalar(uqT2[0:16, 0:2048], uqT2[0:16, 0:2048], 128.0, -64.0,
                                             ALU.mult, ALU.add))
    emit("vector", lambda e: e.tensor_scalar(dnG[0:16, :], dnD[0:16, :], bst[0:16, 4:5], None, ALU.mult))
    emit("vector", lambda e: e.tensor_mul(dnD[0:16, :], dnN[0:16, :], dnN[0:16, :]))
    emit("vector", lambda e: e.tensor_scalar(dnN[0:16, :], dnN[0:16, :], bst[0:16, 3:4], None, ALU.mult))
    emit("vector", lambda e: e.tensor_add(dnG[0:16, :], dnG[0:16, :], dnN[0:16, :]))
    emit("vector", lambda e: e.tensor_scalar(dnG[0:16, :], dnG[0:16, :], bst[0:16, 6:7], None, ALU.add))
    emit("vector", lambda e: e.scalar_tensor_tensor(dnG[0:16, :], dnD[0:16, :], bst[0:16, 12:13],
                                                    dnG[0:16, :], ALU.mult, ALU.add))
    emit("vector", lambda e: e.tensor_scalar(dnG[0:16, :], dnG[0:16, :], 0.0, None, ALU.is_ge))
    ss = INC("sU")
    emit("vector", lambda e, s=ss: e.tensor_scalar(ukT[0:16, :], dnG[0:16, :], 128.0, -64.0,
                                                   ALU.mult, ALU.add).then_inc(s, 1))

    mark("p2")
    SA, SB = psA[:, 0:1536], psB[:, 0:1536]
    ctxE, ctxO = psA[:, 1536:2048], psB[:, 1536:2048]

    def ctxbank(i):
        return ctxE if i % 2 == 0 else ctxO

    sc_hist = {}
    exp_hist = {}
    last_exp_par = {0: 0, 1: 0}
    pbuf_rel = {0: ("sMS", 0), 1: ("sMS", 0)}
    sc_after_head = {}
    xs_cnt = 0
    sh_cnt = 0
    x1b_cnt = 0

    def emit_build(h):
        c, par, pofs = h // 2, h % 2, 64 * (h % 2)
        if h < 2:
            W("sync", "sQE", 64)
            W("sync", "sU", 1)
        else:
            W("sync", "sEV3", (h - 1) * AQB)
        dma("sync", AP(qkhK[par], 0, [[4096, 64], [1, 4096]]),
            AP(qkA, 32768 * pofs + 4096 * c, [[32768, 64], [1, 4096]]), "sBH")
        dma("sync", AP(qkhK[par], 4096 * 64, [[4096, 1], [1, 4096]]),
            AP(ukT, 4096 * h, [[4096, 1], [1, 4096]]), "sBH")
        dma("sync", AP(qkhQ[par], 0, [[2048, 64], [1, 2048]]),
            AP(qkA, 32768 * pofs + 4096 * c, [[32768, 64], [1, 2048]]), "sBH")
        dma("sync", AP(qkhQ[par], 2048 * 64, [[2048, 1], [1, 2048]]),
            AP(uqT2, 2048 * h, [[2048, 1], [1, 2048]]), "sBH")

    emit_build(0)
    emit_build(1)

    au = NH * AQB
    if upto and upto.startswith("attn") and len(upto) > 4:
        au = int(upto[4:])

    W("scalar", "sMS", ms_all)
    for h in range((au + AQB - 1) // AQB):
        c, par = h // 2, h % 2
        for qb in range(min(AQB, au - h * AQB)):
            i = h * AQB + qb
            def emit_ctx_group(gp, h=h, qb=qb, i=i):
                k0p, glp = KC_GROUPS[gp]
                W("tensor", "sEXP", exp_hist[(h, qb, gp)])
                if gp == 0 and i >= 2:
                    W("tensor", "sEV3", i - 1)
                cb = ctxbank(i)
                for j in range(glp):
                    kc = k0p + j
                    ss2 = INC("sCTG") if j == glp - 1 else None
                    st0 = (kc == 0)
                    sp0 = (kc == 31)
                    def mk(kc=kc, j=j, gp=gp, cb=cb, h=h, st0=st0, sp0=sp0, ss2=ss2):
                        def f(e):
                            ins = e.matmul(cb[0:65, :],
                                           AP(vW, 1040 * kc + 65 * h, [[33280, 128], [1, 65]]),
                                           Pbu[gp % 2][:, 512 * j:512 * (j + 1)],
                                           start=st0, stop=sp0)
                            if ss2 is not None:
                                ins.then_inc(ss2, 1)
                        return f
                    emit("tensor", mk())
                pbuf_rel[gp % 2] = ("sCTG", C["sCTG"])

            if qb == 0:
                W("tensor", "sBH", 64 * (h + 1))
            for g, (k0, glen) in enumerate(KC_GROUPS):
                sp = g % 2
                Sps = SA if sp == 0 else SB
                W("tensor", "sEXP", last_exp_par[sp])
                for j in range(glen):
                    kc = k0 + j
                    if j < glen - 1:
                        emit("tensor", lambda e, kc=kc, j=j, Sps=Sps, par=par, qb=qb:
                             e.matmul(Sps[:, 512 * j:512 * (j + 1)],
                                      qkhK[par][0:65, 128 * kc:128 * (kc + 1)],
                                      qkhQ[par][0:65, 512 * qb:512 * (qb + 1)],
                                      start=True, stop=True))
                    else:
                        ss = INC("sSC")
                        emit("tensor", lambda e, kc=kc, j=j, Sps=Sps, par=par, qb=qb, s=ss:
                             e.matmul(Sps[:, 512 * j:512 * (j + 1)],
                                      qkhK[par][0:65, 128 * kc:128 * (kc + 1)],
                                      qkhQ[par][0:65, 512 * qb:512 * (qb + 1)],
                                      start=True, stop=True).then_inc(s, 1))
                sc_hist[(h, qb, g)] = C["sSC"]
                if g == 1 and i >= 1:
                    W("tensor", "sEV1", i)
                    pb = ctxbank(i - 1)
                    dsl = 512 * ((i - 1) % 2)
                    ss = INC("sEV2")
                    emit("tensor", lambda e, pb=pb, dsl=dsl, s=ss:
                         e.matmul(pb[0:64, :], ones_f[64:65, 0:64],
                                  den_sb[64:65, dsl:dsl + 512], start=True, stop=True).then_inc(s, 1))

                if g >= 1:
                    emit_ctx_group(g - 1)
                W("scalar", "sSC", sc_hist[(h, qb, g)])
                rs, rv = pbuf_rel[g % 2]
                W("scalar", rs, rv)
                ss = INC("sEXP")
                emit("scalar", lambda e, g=g, glen=glen, Sps=Sps, s=ss:
                     e.activation(Pbu[g % 2][:, 0:512 * glen], Sps[:, 0:512 * glen],
                                  AF.Exp, bias=bst[:, 10:11], scale=1.0).then_inc(s, 1))
                exp_hist[(h, qb, g)] = C["sEXP"]
                last_exp_par[sp] = C["sEXP"]
            emit_ctx_group(NGRP - 1)

            W("vector", "sCTG", (i + 1) * NGRP)
            cb = ctxbank(i)
            dsl = 512 * (i % 2)
            emit("vector", lambda e, cb=cb, dsl=dsl:
                 e.tensor_scalar(den_sb[64:65, dsl:dsl + 512], cb[64:65, :], 1e-30, None, ALU.add))
            emit("vector", lambda e, dsl=dsl:
                 e.reciprocal(den_sb[64:65, dsl:dsl + 512], den_sb[64:65, dsl:dsl + 512]))
            ss = INC("sEV1")
            emit("vector", lambda e, cb=cb, s=ss:
                 e.tensor_copy(ctmp[0:64, :], cb[0:64, :]).then_inc(s, 1))
            W("vector", "sEV2", i + 1)
            if h % 2 == 0:
                dest = ctxbf(c, qb, 0, 64)
            else:
                dest = otmp[0:64, 0:512]
                if i >= 8:
                    pass
            if h % 2 == 1:
                W("vector", "sSH", 16 * sh_cnt)
            ss = INC("sEV3")
            emit("vector", lambda e, cb=cb, dst=dest, s=ss:
                 e.tensor_mul(dst, ctmp[0:64, :], cb[0:64, :]).then_inc(s, 1))
            if h % 2 == 1:
                if xs_cnt >= 2:
                    W("sync", "sX1B", xs_cnt - 1)
                dma("sync", xsu[xs_cnt % 2][:, 0:512],
                    AP(xTp, 4096 * 128 * c + 512 * qb, [[4096, 128], [1, 512]]), "sXS")
                my_xs = xs_cnt
                xs_cnt += 1
                W("sync", "sEV3", i + 1)
                dma("sync", ctxbf(c, qb, 64, 64), otmp[0:64, 0:512], "sSH")
                sh_cnt += 1
                W("vector", "sSH", 16 * sh_cnt)
                W("vector", "sXS", 16 * (my_xs + 1))
                ss = INC("sX1B")
                emit("vector", lambda e, c=c, qb=qb, xb2=xsu[my_xs % 2], s=ss:
                     e.tensor_add(x1bsl(c, qb), ctxbf(c, qb), xb2[:, 0:512]).then_inc(s, 1))
                x1b_cnt += 1
        sc_after_head[h] = C["sSC"]
        if h + 2 < NH and (h + 2) * AQB < au:
            emit_build(h + 2)
    W("tensor", "sEV1", au)
    pb = ctxbank(au - 1)
    dsl = 512 * ((au - 1) % 2)
    ss = INC("sEV2")
    emit("tensor", lambda e, pb=pb, dsl=dsl, s=ss:
         e.matmul(pb[0:64, :], ones_f[64:65, 0:64], den_sb[64:65, dsl:dsl + 512],
                  start=True, stop=True).then_inc(s, 1))

    mark("attn")
    EXP_TOTAL = C["sEXP"]
    m2bc, rs2bc = psA[:, 1024:1536], psA[:, 1536:2048]
    rps = psB[0:1, 0:512]
    W("tensor", "sX1B", 32)
    W("tensor", "sEXP", EXP_TOTAL)
    W("tensor", "sEV3", NH * AQB)
    emit("vector", lambda e: e.tensor_copy(wrcol[:, 0:8], SMt[:, 48:56]))
    sqf2 = [8 * 4]
    nm2 = [8 * 8]
    for q2 in range(4):
        if q2 >= 1:
            W("tensor", "sL2", 2 * q2)
        for f in range(4):
            W("vector", "sNM", 2 * sqf2[0] - 2)
            for j in range(2):
                cch = 2 * f + j
                xsl = x1bsl(cch, q2)
                if j == 0:
                    emit("vector", lambda e, x=xsl, f=f: e.tensor_mul(sq2h[f % 2][:, 0:512], x, x))
                else:
                    ss = INC("sSQ3")
                    emit("vector", lambda e, x=xsl, f=f, s=ss:
                         e.tensor_mul(sq2h[f % 2][:, 512:1024], x, x).then_inc(s, 1))
            sqf2[0] += 1
            W("tensor", "sSQ3", 2 + 4 * q2 + f + 1)
            for j in range(2):
                cch = 2 * f + j
                ss = INC("sNM")
                emit("tensor", lambda e, cch=cch, f=f, j=j, s=ss:
                     e.matmul(sqq, ones_b[:, 0:1], sq2h[f % 2][:, 512 * j:512 * (j + 1)],
                              start=(cch == 0), stop=(cch == NCH - 1)).then_inc(s, 1))
                nm2[0] += 1
        for cch in range(NCH):
            if cch < NCH - 1:
                emit("tensor", lambda e, cch=cch, q2=q2:
                     e.matmul(stq, ones_b[:, 0:1], x1bsl(cch, q2), start=(cch == 0), stop=False))
            else:
                ss = INC("sST2")
                emit("tensor", lambda e, cch=cch, q2=q2, s=ss:
                     e.matmul(stq, ones_b[:, 0:1], x1bsl(cch, q2), start=False, stop=True).then_inc(s, 1))
        W("vector", "sST2", q2 + 1)
        W("vector", "sNM", 64 + 8 * (q2 + 1))
        if q2 >= 1:
            W("vector", "sL2S", 16 * q2)
        emit("vector", lambda e: e.tensor_scalar(ln2m[0:1, 0:512], stq, 1.0 / D, None, ALU.mult))
        emit("vector", lambda e: e.tensor_scalar(ln2m[0:1, 512:1024], sqq, 1.0 / D, None, ALU.mult))
        emit("vector", lambda e: e.tensor_mul(scrF[0:1, 0:512], ln2m[0:1, 0:512], ln2m[0:1, 0:512]))
        ss = INC("sL2")
        emit("vector", lambda e, s=ss: e.tensor_sub(scrF[0:1, 0:512], ln2m[0:1, 512:1024],
                                                    scrF[0:1, 0:512]).then_inc(s, 1))
        W("scalar", "sL2", 2 * q2 + 1)
        ss = INC("sSS2")
        emit("scalar", lambda e, s=ss: e.activation(scrF[0:1, 0:512], scrF[0:1, 0:512],
                                                    AF.Sqrt, bias=bst[0:1, 9:10], scale=1.0).then_inc(s, 1))
        W("vector", "sSS2", 1 + q2 + 1)
        emit("vector", lambda e: e.reciprocal(scrF[0:1, 0:512], scrF[0:1, 0:512]))
        emit("vector", lambda e, q2=q2: e.tensor_scalar(rowA[0:1, 512 * q2:512 * (q2 + 1)],
                                                        scrF[0:1, 0:512], 1.0, None, ALU.add))
        emit("vector", lambda e: e.tensor_mul(ln2m[0:1, 512:1024], ln2m[0:1, 0:512], scrF[0:1, 0:512]))
        ss = INC("sL2")
        emit("vector", lambda e, s=ss: e.tensor_scalar(ln2m[0:1, 512:1024], ln2m[0:1, 512:1024],
                                                       -1.0, None, ALU.mult).then_inc(s, 1))
        W("sync", "sL2", 2 * (q2 + 1))
        dma("sync", AP(rowA, 4096 * 64 + 512 * q2, [[4096, 1], [1, 512]]),
            AP(ln2m, 512, [[1024, 1], [1, 512]]), "sL2S")
        W("tensor", "sL2", 2 * (q2 + 1))
        if q2 >= 1:
            W("tensor", "sY2", q2)
        emit("tensor", lambda e: e.matmul(m2bc, ones_f[0:1, 0:128], ln2m[0:1, 0:512], start=True, stop=True))
        ss = INC("sBC2")
        emit("tensor", lambda e, s=ss: e.matmul(rs2bc, ones_f[0:1, 0:128], scrF[0:1, 0:512],
                                                start=True, stop=True).then_inc(s, 1))
        W("vector", "sBC2", q2 + 1)
        for cch in range(NCH):
            ysl = y2t[:, 2048 * cch + 512 * q2:2048 * cch + 512 * (q2 + 1)]
            emit("vector", lambda e, cch=cch, q2=q2, y=ysl: e.tensor_sub(y, x1bsl(cch, q2), m2bc))
            if cch < NCH - 1:
                emit("vector", lambda e, y=ysl: e.tensor_mul(y, y, rs2bc))
            else:
                ss = INC("sY2")
                emit("vector", lambda e, y=ysl, s=ss: e.tensor_mul(y, y, rs2bc).then_inc(s, 1))
        W("tensor", "sY2", q2 + 1)
        if q2 >= 1:
            W("tensor", "sRRE", q2)
        for cch in range(NCH):
            if cch < NCH - 1:
                emit("tensor", lambda e, cch=cch, q2=q2:
                     e.matmul(rps, wrcol[:, cch:cch + 1],
                              y2t[:, 2048 * cch + 512 * q2:2048 * cch + 512 * (q2 + 1)],
                              start=(cch == 0), stop=False))
            else:
                ss = INC("sRR")
                emit("tensor", lambda e, cch=cch, q2=q2, s=ss:
                     e.matmul(rps, wrcol[:, cch:cch + 1],
                              y2t[:, 2048 * cch + 512 * q2:2048 * cch + 512 * (q2 + 1)],
                              start=False, stop=True).then_inc(s, 1))
        W("vector", "sRR", q2 + 1)
        ss = INC("sRRE")
        emit("vector", lambda e, q2=q2, s=ss:
             e.tensor_copy(rowA[0:1, 2048 + 512 * q2:2048 + 512 * (q2 + 1)], rps).then_inc(s, 1))

    W("gpsimd", "sRRE", 4)
    dma("gpsimd", cc_in.ap(), AP(rowA, 2048, [[4096, 1], [1, 2048]]), "sCC")
    W("gpsimd", "sCC", C["sCC"])
    ss = INC("sCC")
    emit("gpsimd", lambda e, s=ss: e.collective_compute(
        "AllGather", ALU.bypass, ins=[cc_in.ap()], outs=[cc_out.ap()],
        replica_groups=[[0, 1], [2, 3], [4, 5], [6, 7]]).then_inc(s, 1))
    W("gpsimd", "sCC", C["sCC"])
    dma("gpsimd", rfull[0:32, 0:128], AP(cc_out, 0, [[128, 32], [1, 128]]), "sCC")
    CC_TOTAL = C["sCC"]

    mark("ln2")
    W("vector", "sCC", CC_TOTAL)
    emit("vector", lambda e: e.memset(tbuf[:, :], 0.0))

    def SP():
        emit("vector", lambda e: e.tensor_copy(scrF[64:96, 256:384], ones_f[64:96, 0:128]))

    def part_total(red_op, dst_col):
        emit("vector", lambda e: e.tensor_reduce(tbuf[0:32, 0:1], rfull[:, :], AXX, red_op))
        SP()
        emit("vector", lambda e: e.transpose(tbuf[0:32, 32:64], tbuf[0:32, 0:32]))
        SP()
        emit("vector", lambda e: e.tensor_reduce(bst[0:1, dst_col:dst_col + 1],
                                                 tbuf[0:1, 32:64], AXX, red_op))
        SP()

    part_total(ALU.min, 0)
    emit("vector", lambda e: e.tensor_scalar(bst[0:1, 0:1], bst[0:1, 0:1], -1.0, None, ALU.add))
    part_total(ALU.max, 1)
    emit("vector", lambda e: e.tensor_scalar(bst[0:1, 1:2], bst[0:1, 1:2], 1.0, None, ALU.add))
    SP()
    for _ in range(BIS_ITERS):
        emit("vector", lambda e: e.tensor_add(bst[0:1, 2:3], bst[0:1, 0:1], bst[0:1, 1:2]))
        SP()
        emit("vector", lambda e: e.tensor_scalar(bst[0:1, 2:3], bst[0:1, 2:3], 0.5, None, ALU.mult))
        SP()
        emit("vector", lambda e: e.stream_shuffle(bst[0:32, 6:7], bst[0:32, 2:3], [0] * 32))
        SP()
        emit("vector", lambda e: e.tensor_scalar(scrF[0:32, 0:128], rfull[:, :],
                                                 bst[0:32, 6:7], None, ALU.is_ge))
        emit("vector", lambda e: e.tensor_reduce(tbuf[0:32, 0:1], scrF[0:32, 0:128], AXX, ALU.add))
        SP()
        emit("vector", lambda e: e.transpose(tbuf[0:32, 32:64], tbuf[0:32, 0:32]))
        SP()
        emit("vector", lambda e: e.tensor_reduce(bst[0:1, 3:4], tbuf[0:1, 32:64], AXX, ALU.add))
        SP()
        emit("vector", lambda e: e.tensor_scalar(bsti[0:1, 0:1], bst[0:1, 3:4], float(NQ), None, ALU.is_ge))
        SP()
        emit("vector", lambda e: e.select(bst[0:1, 0:1], bsti[0:1, 0:1], bst[0:1, 2:3], bst[0:1, 0:1]))
        SP()
        emit("vector", lambda e: e.select(bst[0:1, 1:2], bsti[0:1, 0:1], bst[0:1, 1:2], bst[0:1, 2:3]))
        SP()
    ss = INC("sMSG")
    emit("vector", lambda e, s=ss: e.tensor_copy(bst[0:1, 11:12], bst[0:1, 0:1]).then_inc(s, 1))
    for qr in range(4):
        W("scalar", "sMSG", 1 + 2 * qr)
        ss = INC("sMSG")
        emit("scalar", lambda e, qr=qr, s=ss:
             e.activation(scrF[0:1, 0:512], rowA[0:1, 2048 + 512 * qr:2560 + 512 * qr],
                          AF.Sigmoid).then_inc(s, 1))
        W("vector", "sMSG", 2 + 2 * qr)
        emit("vector", lambda e, qr=qr:
             e.tensor_scalar(rowA[0:1, 2048 + 512 * qr:2560 + 512 * qr],
                             rowA[0:1, 2048 + 512 * qr:2560 + 512 * qr],
                             bst[0:1, 0:1], None, ALU.is_ge))
        ss = INC("sMSG")
        emit("vector", lambda e, qr=qr, s=ss:
             e.tensor_mul(rowA[0:1, 2048 + 512 * qr:2560 + 512 * qr],
                          rowA[0:1, 2048 + 512 * qr:2560 + 512 * qr],
                          scrF[0:1, 0:512]).then_inc(s, 1))
    MSG_TOTAL = C["sMSG"]

    mark("bis")
    FQB = 8
    W("gpsimd", "sCTG", NH * AQB * NGRP)
    for oc in range(32):
        dma("gpsimd", AP(W1v, 1024 * oc, [[32768, 128], [1, 1024]]),
            AP(W1C, 1024 * oc, [[32768, 128], [1, 1024]]), "sW1")
    W1_TOTAL = C["sW1"]
    W("gpsimd", "sY2", 4)
    for qbf in range(FQB):
        for dc in range(NCH):
            j = qbf * NCH + dc
            if j >= 2:
                W("gpsimd", "sF2", j - 1)
            dma("gpsimd", AP(W2s[dc % 2][0], 0, [[2048, 128], [1, 2048]]),
                AP(W2C, 4096 * dc, [[32768, 128], [1, 2048]]), "sW2")
            dma("gpsimd", AP(W2s[dc % 2][1], 0, [[2048, 128], [1, 2048]]),
                AP(W2C, 4096 * dc + 2048, [[32768, 128], [1, 2048]]), "sW2")

    hps = [psA[:, 0:256], psA[:, 512:768]]
    f2ps = [psA[:, 1024:1280], psA[:, 1536:1792]]
    Abc, Bbc, Mbc = psB[:, 0:256], psB[:, 512:768], psB[:, 1024:1280]
    W("tensor", "sW1", W1_TOTAL)
    W("tensor", "sY2", 4)
    W("tensor", "sRRE", 4)
    for qbf in range(FQB):
        for oc in range(32):
            ko = qbf * 32 + oc
            if ko >= 2:
                W("tensor", "sGE", ko - 1)
            for ic in range(NCH):
                if ic < NCH - 1:
                    emit("tensor", lambda e, oc=oc, ic=ic, qbf=qbf:
                         e.matmul(hps[oc % 2], W1v[:, 1024 * oc + 128 * ic:1024 * oc + 128 * (ic + 1)],
                                  y2t[:, 2048 * ic + 256 * qbf:2048 * ic + 256 * (qbf + 1)],
                                  start=(ic == 0), stop=False))
                else:
                    ss = INC("sF1")
                    emit("tensor", lambda e, oc=oc, ic=ic, qbf=qbf, s=ss:
                         e.matmul(hps[oc % 2], W1v[:, 1024 * oc + 128 * ic:1024 * oc + 128 * (ic + 1)],
                                  y2t[:, 2048 * ic + 256 * qbf:2048 * ic + 256 * (qbf + 1)],
                                  start=False, stop=True).then_inc(s, 1))
            W("scalar", "sF1", ko + 1)
            ss = INC("sGE")
            emit("scalar", lambda e, oc=oc, s=ss:
                 e.activation(hbf[:, 256 * oc:256 * (oc + 1)], hps[oc % 2],
                              AF.Gelu_apprx_tanh, bias=SMt[:, 16 + oc:17 + oc], scale=1.0).then_inc(s, 1))
        W("tensor", "sMSG", MSG_TOTAL)
        if qbf >= 1:
            W("tensor", "sFE", NCH * qbf)
        emit("tensor", lambda e, qbf=qbf: e.matmul(Abc, ones_f[0:1, 0:128],
                                                   rowA[0:1, 256 * qbf:256 * (qbf + 1)], start=True, stop=True))
        emit("tensor", lambda e, qbf=qbf: e.matmul(Bbc, ones_f[64:65, 0:128],
                                                   AP(rowA, 4096 * 64 + 256 * qbf, [[4096, 1], [1, 256]]),
                                                   start=True, stop=True))
        ss = INC("sFB")
        emit("tensor", lambda e, qbf=qbf, s=ss: e.matmul(Mbc, ones_f[0:1, 0:128],
                                                         rowA[0:1, 2048 + 256 * qbf:2048 + 256 * (qbf + 1)],
                                                         start=True, stop=True).then_inc(s, 1))
        W("tensor", "sGE", 32 * (qbf + 1))
        for dc in range(NCH):
            j = qbf * NCH + dc
            W("tensor", "sW2", 32 * (j + 1))
            if j >= 2:
                W("tensor", "sFE", j - 1)
            for hc in range(32):
                wt = W2s[dc % 2][0] if hc < 16 else W2s[dc % 2][1]
                hcl = 128 * (hc % 16)
                if hc < 31:
                    emit("tensor", lambda e, wt=wt, hcl=hcl, hc=hc, dc=dc:
                         e.matmul(f2ps[dc % 2], wt[:, hcl:hcl + 128],
                                  hbf[:, 256 * hc:256 * (hc + 1)], start=(hc == 0), stop=False))
                else:
                    ss = INC("sF2")
                    emit("tensor", lambda e, wt=wt, hcl=hcl, hc=hc, dc=dc, s=ss:
                         e.matmul(f2ps[dc % 2], wt[:, hcl:hcl + 128],
                                  hbf[:, 256 * hc:256 * (hc + 1)], start=False, stop=True).then_inc(s, 1))
            W("vector", "sF2", j + 1)
            W("vector", "sFB", qbf + 1)
            if dc == 0:
                emit("vector", lambda e: e.tensor_copy(msig_sb[:, 0:256], Mbc))
            W("vector", "sXS", 16 * (32 + j + 1))
            if j >= 2:
                W("vector", "sOUT", 16 * (j - 1))
            ob = outbu[j % 2]
            cslice = AP(qkA, 4096 * dc + 256 * qbf, [[32768, 128], [1, 256]])
            emit("vector", lambda e, ob=ob, cs=cslice, xv=xs2u[j % 2]:
                 e.tensor_add(ob[:, 0:256], xv[:, 0:256], cs))
            emit("vector", lambda e, ob=ob: e.tensor_mul(ob[:, 0:256], ob[:, 0:256], Abc))
            emit("vector", lambda e, ob=ob: e.tensor_add(ob[:, 0:256], ob[:, 0:256], Bbc))
            emit("vector", lambda e, dc=dc:
                 e.scalar_tensor_tensor(ubuf[:, 0:256], f2ps[dc % 2], SMt[:, 8 + dc:9 + dc],
                                        msig_sb[:, 0:256], ALU.add, ALU.mult))
            ss = INC("sFE")
            emit("vector", lambda e, ob=ob, s=ss:
                 e.tensor_add(ob[:, 0:256], ob[:, 0:256], ubuf[:, 0:256]).then_inc(s, 1))
            if j >= 2:
                W("sync", "sFE", j - 1)
            dma("sync", xs2u[j % 2][:, 0:256],
                AP(xTp, 4096 * 128 * dc + 256 * qbf, [[4096, 128], [1, 256]]), "sXS")
            W("sync", "sFE", j + 1)
            dma("sync", out[128 * dc:128 * (dc + 1), 256 * qbf:256 * (qbf + 1)],
                outbu[j % 2][:, 0:256], "sOUT")
    W("sync", "sOUT", C["sOUT"])

    mark("full")
    phkey = upto or "full"
    if phkey.startswith("attn"):
        phkey = "attn"
    lens, Cme = marks[phkey]
    Cm = (None, Cme)
    dump_waits = []
    dumps = []
    if upto == "p0":
        dump_waits = [("sMS", Cme["sMS"])]
        dumps = [(out[0:128, 0:128], bass.AP(SMt, 0, [[128, 128], [1, 128]])),
                 (out[128:256, 0:120], bass.AP(amat, 0, [[240, 128], [1, 120]])),
                 (out[256:384, 0:12], bass.AP(bst, 0, [[14, 128], [1, 12]]))]
    elif upto == "p1":
        dump_waits = [("sQE", Cme["sQE"]), ("sVE", Cme["sVE"])]
        dumps = [(out[0:128, 0:2048], bass.AP(qkA, 0, [[32768, 128], [1, 2048]])),
                 (out[128:256, 0:2048], bass.AP(vW, 0, [[33280, 128], [1, 2048]])),
                 (out[256:384, 0:2048], bass.AP(y1u[1], 0, [[4096, 128], [1, 2048]])),
                 (out[384:512, 0:2048], bass.AP(xbu[1], 0, [[4096, 128], [1, 2048]])),
                 (out[512:513, 0:2048], bass.AP(rowA, 0, [[4096, 1], [1, 2048]])),
                 (out[513:514, 0:2048], bass.AP(rowA, 2048, [[4096, 1], [1, 2048]])),
                 (out[514:515, 0:1024], bass.AP(rowA, 4096, [[4096, 1], [1, 1024]]))]
    elif upto == "p2":
        dump_waits = [("sU", 1)]
        dumps = [(out[0:16, 0:2048], bass.AP(ukT, 0, [[4096, 16], [1, 2048]])),
                 (out[16:32, 0:2048], bass.AP(uqT2, 0, [[2048, 16], [1, 2048]])),
                 (out[32:48, 0:2048], bass.AP(dnN, 0, [[4096, 16], [1, 2048]])),
                 (out[48:64, 0:2048], bass.AP(dnD, 0, [[4096, 16], [1, 2048]])),
                 (out[64:80, 0:12], bass.AP(bst, 0, [[14, 16], [1, 12]]))]
    elif upto == "p2a":
        dump_waits = [("sP2", Cme["sP2"])]
        dumps = [(out[0:16, 0:2048], bass.AP(dnD, 0, [[4096, 16], [1, 2048]])),
                 (out[16:32, 0:2048], bass.AP(dnN, 0, [[4096, 16], [1, 2048]])),
                 (out[32:160, 0:240], bass.AP(amat, 0, [[240, 128], [1, 240]])),
                 (out[160:288, 0:1], bass.AP(aslc, 0, [[1, 128], [1, 1]])),
                 (out[288:416, 0:4], bass.AP(SMt, 56, [[128, 128], [1, 4]])),
                 (out[416:417, 0:12], bass.AP(bst, 0, [[14, 1], [1, 12]]))]
    elif upto and upto.startswith("attn"):
        dump_waits = [("sX1B", Cme["sX1B"]), ("sEV3", Cme["sEV3"]),
                      ("sEV2", Cme["sEV2"]), ("sSH", Cme["sSH"])]
        dumps = [(out[128 * c:128 * (c + 1), 0:2048],
                  bass.AP(qkA, 4096 * c + 2048, [[32768, 128], [512, 4], [1, 512]]))
                 for c in range(au // 8)]
        if au < 64:
            dumps.append((out[896:1024, 0:2048],
                          bass.AP(qkA, 0, [[32768, 128], [1, 2048]])))
    elif upto == "ln2":
        dump_waits = [("sY2", Cme["sY2"]), ("sCC", Cme["sCC"])]
        dumps = [(out[128 * c:128 * (c + 1), 0:2048],
                  bass.AP(y2t, 2048 * c, [[16384, 128], [1, 2048]]))
                 for c in range(8)]
        dumps.append((out[960:992, 0:128], bass.AP(rfull, 0, [[128, 32], [1, 128]])))
        dumps.append((out[992:993, 0:2048], bass.AP(rowA, 2048, [[4096, 1], [1, 2048]])))
    elif upto == "bis":
        dump_waits = [("sMSG", Cm[1]["sMSG"])]
        dumps = [(out[0:1, 0:2048], bass.AP(rowA, 2048, [[4096, 1], [1, 2048]])),
                 (out[1:2, 0:8], bass.AP(bst, 0, [[14, 1], [1, 8]]))]
    if upto and upto != "full":
        pass

    with nc.Block() as block:
        @block.sync
        def _(e):
            for f in Q["sync"][:lens["sync"]]:
                f(e)

        @block.gpsimd
        def _(e):
            for f in Q["gpsimd"][:lens["gpsimd"]]:
                f(e)
            if upto and upto != "full":
                for sn, v in dump_waits:
                    e.wait_ge(sem[sn], v)
                ndum = 0
                for o_, i_ in dumps:
                    e.dma_start(out=o_, in_=i_, allow_slow_non_contiguous=True).then_inc(sem["sOUT"], 16)
                    ndum += 1
                e.wait_ge(sem["sOUT"], Cm[1]["sOUT"] + 16 * ndum)

        @block.tensor
        def _(e):
            for f in Q["tensor"][:lens["tensor"]]:
                f(e)

        @block.vector
        def _(e):
            for f in Q["vector"][:lens["vector"]]:
                f(e)

        @block.scalar
        def _(e):
            for f in Q["scalar"][:lens["scalar"]]:
                f(e)

    for g in reversed(guards):
        g.__exit__(None, None, None)
    psB_g.__exit__(None, None, None)
    psA_g.__exit__(None, None, None)
    return nc


_NC_CACHE = [None]


def kernel(**inputs):
    x = np.ascontiguousarray(np.asarray(inputs["L0_attn_hidden_states"], np.float32))
    qk_w = np.asarray(inputs["L0_attn_qk_w"], np.float32)
    v_w = np.asarray(inputs["L0_attn_v_w"], np.float32)
    W1 = np.asarray(inputs["L0_ffn_W1"], np.float32)
    W2 = np.asarray(inputs["L0_ffn_W2"], np.float32)
    com = {
        "qkwC": np.ascontiguousarray(qk_w.T.reshape(8, 128, 8, 128).transpose(1, 2, 0, 3).reshape(128, 8192)),
        "vwT": np.ascontiguousarray(v_w.T),
        "vb": np.ascontiguousarray(np.asarray(inputs["L0_attn_v_b"], np.float32)),
        "W1C": np.ascontiguousarray(W1.reshape(8, 128, 32, 128).transpose(1, 2, 0, 3).reshape(128, 32768)),
        "W2C": np.ascontiguousarray(W2.reshape(32, 128, 8, 128).transpose(1, 2, 0, 3).reshape(128, 32768)),
    }
    SM = np.zeros((128, 128), np.float32)
    SM[:, 0:8] = np.asarray(inputs["L0_attn_qk_b"], np.float32).reshape(8, 128).T
    SM[:, 8:16] = np.asarray(inputs["L0_ffn_b2"], np.float32).reshape(8, 128).T
    SM[:, 16:48] = np.asarray(inputs["L0_ffn_b1"], np.float32).reshape(32, 128).T
    SM[:, 48:56] = np.asarray(inputs["L0_ffn_w_router"], np.float32).reshape(8, 128).T
    hav = np.asarray(inputs["L0_attn_hash_a"], np.float32)
    SM[:, 56] = np.concatenate([hav[:64]] * 2)
    SM[:, 57] = hav[64]
    SM[:, 58] = hav[65]
    com["SMd"] = np.ascontiguousarray(SM)
    in_maps = []
    for core in range(8):
        b, qh = core // 2, core % 2
        xb = x[b]
        own = xb[2048 * qh:2048 * (qh + 1)]
        oth = xb[2048 * (1 - qh):2048 * (2 - qh)]
        xperm = np.concatenate([own, oth], axis=0)
        m = dict(com)
        m["xTp"] = np.ascontiguousarray(xperm.T)
        m["xQ"] = np.ascontiguousarray(
            xperm.reshape(8, 512, 8, 128).transpose(3, 0, 2, 1).reshape(128, 32768))
        in_maps.append(m)

    if _NC_CACHE[0] is None:
        _NC_CACHE[0] = build_program()
    nc = _NC_CACHE[0]
    res = run_bass_kernel_spmd(nc, in_maps, core_ids=list(range(8)))
    outp = np.empty((4, 4096, 1024), np.float32)
    for core in range(8):
        b, qh = core // 2, core % 2
        outp[b, 2048 * qh:2048 * (qh + 1)] = res.results[core]["out"].T
    return outp
